# version 01 (final)
# baseline (speedup 1.0000x reference)
import numpy as np
import ml_dtypes
import concourse.bass as bass
import concourse.mybir as mybir
from concourse.bass_utils import run_bass_kernel_spmd

F32 = mybir.dt.float32
BF16 = mybir.dt.bfloat16
AF = mybir.ActivationFunctionType
ALU = mybir.AluOpType
AX = mybir.AxisListType
BF = ml_dtypes.bfloat16
NC = 8


class Prog:
    def __init__(self):
        self.nc = bass.Bass("TRN2", target_bir_lowering=False)
        self.ops = []

    def op(self, eng, fn, inc=1):
        self.ops.append((eng, fn, inc))

    def dma(self, out, in_):
        self.op("sync", lambda e, o=out, i=in_: e.dma_start(out=o, in_=i), 16)

    def mm(self, out, lhsT, rhs, start=True, stop=True):
        self.op("tensor", lambda e, o=out, l=lhsT, r=rhs, s=start, p=stop:
                e.matmul(o, l, r, start=s, stop=p))

    def trans(self, out, in_, ident):
        self.op("tensor", lambda e, o=out, i=in_, d=ident: e.transpose(o, i, d))

    def act(self, out, in_, func, bias=None, scale=None):
        src = in_
        if scale is not None:
            self.ts(out, src, float(scale), ALU.mult)
            src = out
        if bias is not None:
            self.ts(out, src, float(bias), ALU.add)
            src = out
        if func == AF.Copy and src is out:
            return
        self.op("scalar", lambda e, o=out, i=src, f=func: e.activation(o, i, f))

    def tt(self, out, a, b, alu):
        self.op("vector", lambda e, o=out, x=a, y=b, p=alu: e.tensor_tensor(o, x, y, p))

    def ts(self, out, a, s, alu):
        self.op("vector", lambda e, o=out, x=a, y=s, p=alu:
                e.tensor_scalar(o, x, y, None, p))

    def cp(self, out, in_):
        self.op("vector", lambda e, o=out, i=in_: e.tensor_copy(o, i))

    def red(self, out, in_, alu):
        if alu == "sum":
            self.op("vector", lambda e, o=out, i=in_: e.reduce_sum(o, i, AX.X))
        else:
            self.op("vector", lambda e, o=out, i=in_: e.reduce_max(o, i, AX.X))

    def finalize(self):
        nc = self.nc
        pres = []
        c = 0
        for (_, _, inc) in self.ops:
            pres.append(c)
            c += inc
        total = c
        with nc.semaphore("GS") as S, nc.Block() as block:
            def mk(name):
                def body(eng):
                    for (en, fn, inc), pre in zip(self.ops, pres):
                        if en == name:
                            eng.wait_ge(S, pre)
                            fn(eng).then_inc(S, inc)
                    if name == "sync":
                        eng.wait_ge(S, total)
                return body
            block.sync(mk("sync"))
            block.tensor(mk("tensor"))
            block.vector(mk("vector"))
            block.scalar(mk("scalar"))
        return nc


def _c(a):
    return np.ascontiguousarray(a)


def build_prep():
    p = Prog()
    nc = p.nc
    specs = [("xt", 4096, 2048), ("w", 4096, 2560), ("ow", 1024, 4096),
             ("bw", 4096, 4096), ("rt", 4096, 16), ("at", 4096, 64),
             ("bt", 1024, 4096)]
    ins, outs = {}, {}
    for nm, r, cdim in specs:
        ins[nm] = nc.dram_tensor(nm + "_f", [r, cdim], F32, kind="ExternalInput").ap()
        outs[nm] = nc.dram_tensor(nm + "_b", [r, cdim], BF16, kind="ExternalOutput").ap()
    ctx = [nc.sbuf_tensor("pin", [128, 4096], F32),
           nc.sbuf_tensor("pout", [128, 4096], BF16)]
    tin = ctx[0].__enter__(); tout = ctx[1].__enter__()
    for nm, r, cdim in specs:
        for i in range(r // 128):
            p.dma(tin[:, 0:cdim], ins[nm][i * 128:(i + 1) * 128, :])
            p.cp(tout[:, 0:cdim], tin[:, 0:cdim])
            p.dma(outs[nm][i * 128:(i + 1) * 128, :], tout[:, 0:cdim])
    nc_ = p.finalize()
    for c in reversed(ctx):
        c.__exit__(None, None, None)
    return nc_


def build_A():
    p = Prog()
    nc = p.nc
    w = nc.dram_tensor("w_b", [4096, 1280], BF16, kind="ExternalInput").ap()
    xt = nc.dram_tensor("xt_b", [4096, 1024], BF16, kind="ExternalInput").ap()
    zxb = nc.dram_tensor("zxb", [1280, 1024], F32, kind="ExternalOutput").ap()
    ctx = [nc.sbuf_tensor("wsb", [128, 32 * 1280], BF16),
           nc.sbuf_tensor("xsb", [128, 32 * 1024], BF16),
           nc.sbuf_tensor("stg", [128, 512], F32),
           nc.psum_tensor("pmm", [128, 512], F32)]
    wsb, xsb, stg, pmm = [c.__enter__() for c in ctx]
    for k in range(32):
        p.dma(wsb[:, k * 1280:(k + 1) * 1280], w[k * 128:(k + 1) * 128, :])
        p.dma(xsb[:, k * 1024:(k + 1) * 1024], xt[k * 128:(k + 1) * 128, :])
    for f in range(10):
        for t in range(2):
            for k in range(32):
                p.mm(pmm[:, :], wsb[:, k * 1280 + f * 128: k * 1280 + (f + 1) * 128],
                     xsb[:, k * 1024 + t * 512: k * 1024 + (t + 1) * 512],
                     start=(k == 0), stop=(k == 31))
            p.cp(stg[:, :], pmm[:, :])
            p.dma(zxb[f * 128:(f + 1) * 128, t * 512:(t + 1) * 512], stg[:, :])
    nc_ = p.finalize()
    for c in reversed(ctx):
        c.__exit__(None, None, None)
    return nc_


def build_B():
    p = Prog()
    nc = p.nc
    zxb = nc.dram_tensor("zxb", [2320, 1027], F32, kind="ExternalInput").ap()
    cw = nc.dram_tensor("cw", [128, 40], F32, kind="ExternalInput").ap()
    cb = nc.dram_tensor("cb", [128, 10], F32, kind="ExternalInput").ap()
    acol_i = nc.dram_tensor("acol", [16, 1], F32, kind="ExternalInput").ap()
    dtb_i = nc.dram_tensor("dtb", [16, 1], F32, kind="ExternalInput").ap()
    drow_i = nc.dram_tensor("drow", [1, 16], F32, kind="ExternalInput").ap()
    st_in = nc.dram_tensor("st_in", [128, 1024], F32, kind="ExternalInput").ap()
    id_i = nc.dram_tensor("ident", [128, 128], F32, kind="ExternalInput").ap()
    mk_i = nc.dram_tensor("mask", [128, 128], F32, kind="ExternalInput").ap()
    o1_i = nc.dram_tensor("ones1", [1, 128], F32, kind="ExternalInput").ap()
    e127_i = nc.dram_tensor("e127", [128, 1], F32, kind="ExternalInput").ap()
    yg = nc.dram_tensor("yg", [1024, 1024], F32, kind="ExternalOutput").ap()
    pssq = nc.dram_tensor("pssq", [128, 8], F32, kind="ExternalOutput").ap()
    st_out = nc.dram_tensor("st_out", [128, 1024], F32, kind="ExternalOutput").ap()
    ctx = {}
    def sb(nm, sh, dt=F32):
        ctx[nm] = nc.sbuf_tensor(nm, sh, dt)
        return ctx[nm].__enter__()
    def ps(nm, sh):
        ctx[nm] = nc.psum_tensor(nm, sh, F32)
        return ctx[nm].__enter__()
    z = sb("z", [128, 19 * 1027])
    xc = sb("xc", [128, 10 * 1024])
    cwsb = sb("cwsb", [128, 40]); cbsb = sb("cbsb", [128, 10])
    acol = sb("acolS", [16, 1]); dtb = sb("dtbS", [16, 1]); drow = sb("drowS", [1, 16])
    dtt = sb("dtt", [16, 1024]); dts = sb("dts", [16, 1024]); dA = sb("dA", [16, 1024])
    tl1 = sb("tl1", [16, 1024]); tl2 = sb("tl2", [16, 1024])
    S = sb("S", [128, 1024]); Stmp = sb("Stmp", [128, 64])
    ident = sb("identS", [128, 128]); mask = sb("maskS", [128, 128])
    o1 = sb("o1S", [1, 128]); e127 = sb("e127S", [128, 1])
    Dbc = sb("Dbc", [128, 16])
    dtT = sb("dtT", [128, 16]); dAT = sb("dAT", [128, 16])
    csT = sb("csT", [128, 16]); ecsT = sb("ecsT", [128, 16]); endbc = sb("endbc", [128, 16])
    rsb = sb("rsb", [1, 16]); csrow = sb("csrow", [1, 128])
    Te = sb("Te", [128, 128]); Msb = sb("Msb", [128, 128]); Gsb = sb("Gsb", [128, 128])
    Btm = sb("Btm", [128, 128]); xh = sb("xh", [128, 64]); xdt = sb("xdt", [128, 64])
    t64a = sb("t64a", [128, 64]); t64b = sb("t64b", [128, 64])
    ztm = sb("ztm", [128, 1024]); zsil = sb("zsil", [128, 1024])
    ytm = sb("ytm", [128, 1024]); ygt = sb("ygt", [128, 1024]); ysq = sb("ysq", [128, 1024])
    tmpc = sb("tmpc", [128, 1024]); pq = sb("pq", [128, 8])
    p_tr = ps("p_tr", [128, 128]); p_sm = ps("p_sm", [128, 16])
    p_bc = ps("p_bc", [128, 128]); p_g = ps("p_g", [128, 128])
    p_z = ps("p_z", [128, 128]); p_y = ps("p_y", [128, 64])
    p_s = ps("p_s", [128, 64]); p_yo = ps("p_yo", [128, 64])

    for i in range(18):
        p.dma(z[:, i * 1027:(i + 1) * 1027], zxb[i * 128:(i + 1) * 128, :])
    p.dma(z[0:16, 18 * 1027:19 * 1027], zxb[2304:2320, :])
    p.dma(cwsb[:, :], cw); p.dma(cbsb[:, :], cb)
    p.dma(acol[:, :], acol_i); p.dma(dtb[:, :], dtb_i); p.dma(drow[:, :], drow_i)
    p.dma(S[:, :], st_in)
    p.dma(ident[:, :], id_i); p.dma(mask[:, :], mk_i)
    p.dma(o1[:, :], o1_i); p.dma(e127[:, :], e127_i)
    p.mm(p_bc[:, 0:16], o1[:, :], drow[:, :])
    p.cp(Dbc[:, :], p_bc[:, 0:16])
    p.act(acol[:, :], acol[:, :], AF.Exp)
    p.ts(acol[:, :], acol[:, :], -1.0, ALU.mult)
    draw = z[0:16, 18 * 1027 + 3: 18 * 1027 + 1027]
    p.ts(dtt[:, :], draw, dtb[:, :], ALU.add)
    p.act(tl1[:, :], dtt[:, :], AF.Abs)
    p.act(tl1[:, :], tl1[:, :], AF.Exp, scale=-1.0)
    p.act(tl1[:, :], tl1[:, :], AF.Ln, bias=1.0)
    p.ts(tl2[:, :], dtt[:, :], 0.0, ALU.max)
    p.tt(dts[:, :], tl2[:, :], tl1[:, :], ALU.add)
    p.ts(dA[:, :], dts[:, :], acol[:, :], ALU.mult)
    for i in range(10):
        src = z[:, (8 + i) * 1027:(9 + i) * 1027]
        dst = xc[:, i * 1024:(i + 1) * 1024]
        p.ts(dst, src[:, 0:1024], cwsb[:, 4 * i:4 * i + 1], ALU.mult)
        for j in range(1, 4):
            p.ts(tmpc[:, :], src[:, j:j + 1024], cwsb[:, 4 * i + j:4 * i + j + 1], ALU.mult)
            p.tt(dst, dst, tmpc[:, :], ALU.add)
        p.ts(dst, dst, cbsb[:, i:i + 1], ALU.add)
        p.act(dst, dst, AF.Silu)
    for c in range(8):
        t0 = c * 128
        p.trans(p_tr[:, 0:16], dts[:, t0:t0 + 128], ident[0:16, 0:16])
        p.cp(dtT[:, :], p_tr[:, 0:16])
        p.trans(p_tr[:, 0:16], dA[:, t0:t0 + 128], ident[0:16, 0:16])
        p.cp(dAT[:, :], p_tr[:, 0:16])
        p.mm(p_sm[:, :], mask[:, :], dAT[:, :])
        p.cp(csT[:, :], p_sm[:, :])
        p.act(ecsT[:, :], csT[:, :], AF.Exp)
        p.mm(p_bc[0:1, 0:16], e127[:, :], csT[:, :])
        p.cp(rsb[:, :], p_bc[0:1, 0:16])
        p.mm(p_g[:, 0:16], o1[:, :], rsb[:, :])
        p.act(endbc[:, :], p_g[:, 0:16], AF.Exp)
        Bfm = xc[:, 8 * 1024 + t0: 8 * 1024 + t0 + 128]
        Cfm = xc[:, 9 * 1024 + t0: 9 * 1024 + t0 + 128]
        p.mm(p_g[:, :], Bfm, Cfm)
        p.cp(Gsb[:, :], p_g[:, :])
        p.trans(p_tr[:, :], Bfm, ident[:, :])
        p.cp(Btm[:, :], p_tr[:, :])
        for zi in range(8):
            p.trans(p_z[:, :], z[:, zi * 1027 + 3 + t0: zi * 1027 + 3 + t0 + 128], ident[:, :])
            p.cp(ztm[:, zi * 128:(zi + 1) * 128], p_z[:, :])
        p.act(zsil[:, :], ztm[:, :], AF.Silu)
        for h in range(16):
            p.trans(p_tr[0:1, :], csT[:, h:h + 1], ident[:, :])
            p.cp(csrow[:, :], p_tr[0:1, :])
            p.mm(p_bc[:, :], o1[:, :], csrow[:, :])
            p.ts(Te[:, :], p_bc[:, :], csT[:, h:h + 1], ALU.subtract)
            p.ts(Te[:, :], Te[:, :], 0.0, ALU.min)
            p.act(Te[:, :], Te[:, :], AF.Exp)
            p.tt(Te[:, :], Te[:, :], mask[:, :], ALU.mult)
            p.tt(Msb[:, :], Te[:, :], Gsb[:, :], ALU.mult)
            ro = 64 * (h % 2)
            p.trans(p_tr[:, 0:64], xc[ro:ro + 64, (h // 2) * 1024 + t0:(h // 2) * 1024 + t0 + 128],
                    ident[ro:ro + 64, ro:ro + 64])
            p.cp(xh[:, :], p_tr[:, 0:64])
            p.ts(xdt[:, :], xh[:, :], dtT[:, h:h + 1], ALU.mult)
            p.mm(p_y[:, :], Msb[:, :], xdt[:, :])
            p.ts(t64a[:, :], xdt[:, :], Te[:, 127:128], ALU.mult)
            p.mm(p_s[:, :], Btm[:, :], t64a[:, :])
            p.mm(p_yo[:, :], Cfm, S[:, 64 * h:64 * h + 64])
            p.ts(t64a[:, :], p_yo[:, :], ecsT[:, h:h + 1], ALU.mult)
            p.tt(t64b[:, :], p_y[:, :], t64a[:, :], ALU.add)
            p.ts(t64a[:, :], xh[:, :], Dbc[:, h:h + 1], ALU.mult)
            p.tt(ytm[:, 64 * h:64 * h + 64], t64b[:, :], t64a[:, :], ALU.add)
            p.ts(Stmp[:, :], S[:, 64 * h:64 * h + 64], endbc[:, h:h + 1], ALU.mult)
            p.tt(S[:, 64 * h:64 * h + 64], Stmp[:, :], p_s[:, :], ALU.add)
        p.tt(ygt[:, :], ytm[:, :], zsil[:, :], ALU.mult)
        p.act(ysq[:, :], ygt[:, :], AF.Square)
        p.red(pq[:, c:c + 1], ysq[:, :], "sum")
        p.dma(yg[t0:t0 + 128, :], ygt[:, :])
    p.dma(st_out, S[:, :])
    p.dma(pssq, pq[:, :])
    nc_ = p.finalize()
    for c in reversed(list(ctx.values())):
        c.__exit__(None, None, None)
    return nc_


def build_D():
    p = Prog()
    nc = p.nc
    yfm = nc.dram_tensor("yfm", [1024, 1024], F32, kind="ExternalInput").ap()
    sq = nc.dram_tensor("sq8", [8, 1024], F32, kind="ExternalInput").ap()
    nw = nc.dram_tensor("nwfm", [128, 8], F32, kind="ExternalInput").ap()
    ow = nc.dram_tensor("ow_b", [1024, 4096], BF16, kind="ExternalInput").ap()
    o8 = nc.dram_tensor("ones8", [8, 128], F32, kind="ExternalInput").ap()
    pout = nc.dram_tensor("pout", [1024, 4096], F32, kind="ExternalOutput").ap()
    ctx = {}
    def sb(nm, sh, dt=F32):
        ctx[nm] = nc.sbuf_tensor(nm, sh, dt)
        return ctx[nm].__enter__()
    ysb = sb("ysb", [128, 8192]); owsb = sb("owsb", [128, 8 * 4096], BF16)
    sqsb = sb("sqsb", [8, 1024]); nwsb = sb("nwsb", [128, 8]); o8sb = sb("o8sb", [8, 128])
    rstd = sb("rstd", [128, 1024]); ytmp = sb("ytmp", [128, 1024])
    ynb = sb("ynb", [128, 8192], BF16); stg = sb("stg", [128, 512])
    ctx["pm"] = nc.psum_tensor("pm", [128, 512], F32); pm = ctx["pm"].__enter__()
    for k in range(8):
        p.dma(ysb[:, k * 1024:(k + 1) * 1024], yfm[k * 128:(k + 1) * 128, :])
        p.dma(owsb[:, k * 4096:(k + 1) * 4096], ow[k * 128:(k + 1) * 128, :])
    p.dma(sqsb[:, :], sq); p.dma(nwsb[:, :], nw); p.dma(o8sb[:, :], o8)
    for s2 in range(2):
        p.mm(pm[:, :], o8sb[:, :], sqsb[:, s2 * 512:(s2 + 1) * 512])
        p.act(rstd[:, s2 * 512:(s2 + 1) * 512], pm[:, :], AF.Ln, bias=1e-5, scale=1.0 / 8192.0)
        p.act(rstd[:, s2 * 512:(s2 + 1) * 512], rstd[:, s2 * 512:(s2 + 1) * 512],
              AF.Exp, scale=-0.5)
    for k in range(8):
        p.tt(ytmp[:, :], ysb[:, k * 1024:(k + 1) * 1024], rstd[:, :], ALU.mult)
        p.ts(ytmp[:, :], ytmp[:, :], nwsb[:, k:k + 1], ALU.mult)
        p.cp(ynb[:, k * 1024:(k + 1) * 1024], ytmp[:, :])
    for t in range(8):
        for o in range(8):
            for k in range(8):
                p.mm(pm[:, :], ynb[:, k * 1024 + t * 128: k * 1024 + (t + 1) * 128],
                     owsb[:, k * 4096 + o * 512: k * 4096 + (o + 1) * 512],
                     start=(k == 0), stop=(k == 7))
            p.cp(stg[:, :], pm[:, :])
            p.dma(pout[t * 128:(t + 1) * 128, o * 512:(o + 1) * 512], stg[:, :])
    nc_ = p.finalize()
    for c in reversed(list(ctx.values())):
        c.__exit__(None, None, None)
    return nc_


def build_E():
    p = Prog()
    nc = p.nc
    ps_ = [nc.dram_tensor(f"pp{i}", [1024, 4096], F32, kind="ExternalInput").ap()
           for i in range(8)]
    xin = nc.dram_tensor("xblk", [1024, 4096], F32, kind="ExternalInput").ap()
    x1 = nc.dram_tensor("x1", [1024, 4096], F32, kind="ExternalOutput").ap()
    x1b = nc.dram_tensor("x1b", [1024, 4096], BF16, kind="ExternalOutput").ap()
    ctx = {}
    def sb(nm, sh, dt=F32):
        ctx[nm] = nc.sbuf_tensor(nm, sh, dt)
        return ctx[nm].__enter__()
    acc = sb("acc", [128, 4096]); ld = sb("ld", [128, 4096]); tmp = sb("tmpE", [128, 4096])
    r1 = sb("r1", [128, 1]); r2 = sb("r2", [128, 1]); xb = sb("xb", [128, 4096], BF16)
    for t in range(8):
        sl = slice(t * 128, (t + 1) * 128)
        p.dma(ld[:, :], ps_[0][sl, :])
        p.cp(acc[:, :], ld[:, :])
        for i in range(1, 8):
            p.dma(ld[:, :], ps_[i][sl, :])
            p.tt(acc[:, :], acc[:, :], ld[:, :], ALU.add)
        p.dma(ld[:, :], xin[sl, :])
        p.tt(acc[:, :], acc[:, :], ld[:, :], ALU.add)
        p.tt(acc[:, :], acc[:, :], ld[:, :], ALU.add)
        p.red(r1[:, :], acc[:, :], "sum")
        p.act(r1[:, :], r1[:, :], AF.Copy, scale=1.0 / 4096.0)
        p.ts(acc[:, :], acc[:, :], r1[:, :], ALU.subtract)
        p.act(tmp[:, :], acc[:, :], AF.Square)
        p.red(r2[:, :], tmp[:, :], "sum")
        p.act(r2[:, :], r2[:, :], AF.Ln, bias=1e-5, scale=1.0 / 4096.0)
        p.act(r2[:, :], r2[:, :], AF.Exp, scale=-0.5)
        p.ts(acc[:, :], acc[:, :], r2[:, :], ALU.mult)
        p.dma(x1[sl, :], acc[:, :])
        p.cp(xb[:, :], acc[:, :])
        p.dma(x1b[sl, :], xb[:, :])
    nc_ = p.finalize()
    for c in reversed(list(ctx.values())):
        c.__exit__(None, None, None)
    return nc_


def build_F1():
    p = Prog()
    nc = p.nc
    xt = nc.dram_tensor("x1t_b", [4096, 1024], BF16, kind="ExternalInput").ap()
    bw = nc.dram_tensor("bw_b", [4096, 4096], BF16, kind="ExternalInput").ap()
    bb = nc.dram_tensor("bb", [1, 4096], F32, kind="ExternalInput").ap()
    o1 = nc.dram_tensor("ones1", [1, 128], F32, kind="ExternalInput").ap()
    bo = nc.dram_tensor("bo", [1024, 4096], F32, kind="ExternalOutput").ap()
    ctx = {}
    def sb(nm, sh, dt=F32):
        ctx[nm] = nc.sbuf_tensor(nm, sh, dt)
        return ctx[nm].__enter__()
    xsb = sb("xsb", [128, 32 * 1024], BF16); bwsb = sb("bwsb", [128, 32 * 512], BF16)
    bbsb = sb("bbsb", [1, 4096]); o1sb = sb("o1sb", [1, 128])
    bbc = sb("bbc", [128, 512]); stg = sb("stgF", [128, 512])
    ctx["pm"] = nc.psum_tensor("pmF", [128, 512], F32); pm = ctx["pm"].__enter__()
    ctx["pb"] = nc.psum_tensor("pbF", [128, 512], F32); pb = ctx["pb"].__enter__()
    for k in range(32):
        p.dma(xsb[:, k * 1024:(k + 1) * 1024], xt[k * 128:(k + 1) * 128, :])
    p.dma(bbsb[:, :], bb); p.dma(o1sb[:, :], o1)
    for o in range(8):
        for k in range(32):
            p.dma(bwsb[:, k * 512:(k + 1) * 512],
                  bw[k * 128:(k + 1) * 128, o * 512:(o + 1) * 512])
        p.mm(pb[:, :], o1sb[:, :], bbsb[:, o * 512:(o + 1) * 512])
        p.cp(bbc[:, :], pb[:, :])
        for t in range(8):
            for k in range(32):
                p.mm(pm[:, :], xsb[:, k * 1024 + t * 128: k * 1024 + (t + 1) * 128],
                     bwsb[:, k * 512:(k + 1) * 512], start=(k == 0), stop=(k == 31))
            p.tt(stg[:, :], pm[:, :], bbc[:, :], ALU.add)
            p.dma(bo[t * 128:(t + 1) * 128, o * 512:(o + 1) * 512], stg[:, :])
    nc_ = p.finalize()
    for c in reversed(list(ctx.values())):
        c.__exit__(None, None, None)
    return nc_


def build_F2a():
    p = Prog()
    nc = p.nc
    xt = nc.dram_tensor("x1t_b", [4096, 1024], BF16, kind="ExternalInput").ap()
    rt = nc.dram_tensor("rt_b", [4096, 16], BF16, kind="ExternalInput").ap()
    at = nc.dram_tensor("at_b", [4096, 64], BF16, kind="ExternalInput").ap()
    id_i = nc.dram_tensor("ident", [128, 128], F32, kind="ExternalInput").ap()
    rhw = nc.dram_tensor("rhw", [1024, 1024], BF16, kind="ExternalOutput").ap()
    ctx = {}
    def sb(nm, sh, dt=F32):
        ctx[nm] = nc.sbuf_tensor(nm, sh, dt)
        return ctx[nm].__enter__()
    xsb = sb("xsb2", [128, 32 * 1024], BF16); rtsb = sb("rtsb", [128, 32 * 16], BF16)
    atsb = sb("atsb", [128, 32 * 64], BF16); ident = sb("idS", [128, 128])
    probs = sb("probs", [128, 16]); accr = sb("accr", [128, 16]); t16 = sb("t16", [128, 16])
    wsb = sb("wsb2", [128, 16]); rh = sb("rh", [128, 64]); rw = sb("rw", [128, 64])
    rr = sb("rr", [128, 1]); rr2 = sb("rr2", [128, 1]); stT = sb("stT", [64, 128], BF16)
    ctx["pl"] = nc.psum_tensor("pl", [128, 16], F32); pl = ctx["pl"].__enter__()
    ctx["ph"] = nc.psum_tensor("ph", [128, 64], F32); ph = ctx["ph"].__enter__()
    ctx["pt"] = nc.psum_tensor("ptr2", [128, 128], F32); pt = ctx["pt"].__enter__()
    for k in range(32):
        p.dma(xsb[:, k * 1024:(k + 1) * 1024], xt[k * 128:(k + 1) * 128, :])
        p.dma(rtsb[:, k * 16:(k + 1) * 16], rt[k * 128:(k + 1) * 128, :])
        p.dma(atsb[:, k * 64:(k + 1) * 64], at[k * 128:(k + 1) * 128, :])
    p.dma(ident[:, :], id_i)
    for t in range(8):
        for k in range(32):
            p.mm(pl[:, :], xsb[:, k * 1024 + t * 128: k * 1024 + (t + 1) * 128],
                 rtsb[:, k * 16:(k + 1) * 16], start=(k == 0), stop=(k == 31))
        for k in range(32):
            p.mm(ph[:, :], xsb[:, k * 1024 + t * 128: k * 1024 + (t + 1) * 128],
                 atsb[:, k * 64:(k + 1) * 64], start=(k == 0), stop=(k == 31))
        p.op("vector", lambda e, o=rr[:, :], i=pl[:, :]: e.reduce_max(o, i, AX.X))
        p.ts(probs[:, :], pl[:, :], rr[:, :], ALU.subtract)
        p.act(probs[:, :], probs[:, :], AF.Exp)
        p.red(rr[:, :], probs[:, :], "sum")
        p.op("vector", lambda e, o=rr2[:, :], i=rr[:, :]: e.reciprocal(o, i))
        p.ts(probs[:, :], probs[:, :], rr2[:, :], ALU.mult)
        for ep in range(16):
            p.ts(t16[:, :], probs[:, :], probs[:, ep:ep + 1], ALU.is_gt)
            if ep == 0:
                p.cp(accr[:, :], t16[:, :])
            else:
                p.tt(accr[:, :], accr[:, :], t16[:, :], ALU.add)
        p.ts(t16[:, :], accr[:, :], 13.5, ALU.is_ge)
        p.tt(wsb[:, :], probs[:, :], t16[:, :], ALU.mult)
        p.red(rr[:, :], wsb[:, :], "sum")
        p.act(rr[:, :], rr[:, :], AF.Copy, bias=1e-6)
        p.op("vector", lambda e, o=rr2[:, :], i=rr[:, :]: e.reciprocal(o, i))
        p.ts(wsb[:, :], wsb[:, :], rr2[:, :], ALU.mult)
        p.cp(rh[:, :], ph[:, :])
        for e_ in range(16):
            p.ts(rw[:, :], rh[:, :], wsb[:, e_:e_ + 1], ALU.mult)
            p.trans(pt[0:64, :], rw[:, :], ident[:, :])
            p.cp(stT[:, :], pt[0:64, :])
            p.dma(rhw[e_ * 64:(e_ + 1) * 64, t * 128:(t + 1) * 128], stT[:, :])
    nc_ = p.finalize()
    for c in reversed(list(ctx.values())):
        c.__exit__(None, None, None)
    return nc_


def build_F2b():
    p = Prog()
    nc = p.nc
    rhw = nc.dram_tensor("rhw", [1024, 1024], BF16, kind="ExternalInput").ap()
    bt = nc.dram_tensor("bt_b", [1024, 4096], BF16, kind="ExternalInput").ap()
    bo = nc.dram_tensor("bo", [1024, 4096], F32, kind="ExternalInput").ap()
    x1 = nc.dram_tensor("x1", [1024, 4096], F32, kind="ExternalInput").ap()
    xo = nc.dram_tensor("xo", [1024, 4096], F32, kind="ExternalOutput").ap()
    ctx = {}
    def sb(nm, sh, dt=F32):
        ctx[nm] = nc.sbuf_tensor(nm, sh, dt)
        return ctx[nm].__enter__()
    rwsb = sb("rwsb", [64, 16 * 1024], BF16); btsb = sb("btsb", [64, 16 * 512], BF16)
    ssb = sb("ssb", [128, 4096]); ld = sb("ldB", [128, 4096]); ld2 = sb("ld2B", [128, 4096])
    tmp = sb("tmpB", [128, 4096]); r1 = sb("r1B", [128, 1]); r2 = sb("r2B", [128, 1])
    ctx["pm"] = nc.psum_tensor("pmB", [128, 512], F32); pm = ctx["pm"].__enter__()
    for e_ in range(16):
        p.dma(rwsb[:, e_ * 1024:(e_ + 1) * 1024], rhw[e_ * 64:(e_ + 1) * 64, :])
    for t in range(8):
        sl = slice(t * 128, (t + 1) * 128)
        p.dma(ld[:, :], bo[sl, :])
        p.dma(ld2[:, :], x1[sl, :])
        for o in range(8):
            for e_ in range(16):
                p.dma(btsb[:, e_ * 512:(e_ + 1) * 512],
                      bt[e_ * 64:(e_ + 1) * 64, o * 512:(o + 1) * 512])
            for e_ in range(16):
                p.mm(pm[:, :], rwsb[:, e_ * 1024 + t * 128: e_ * 1024 + (t + 1) * 128],
                     btsb[:, e_ * 512:(e_ + 1) * 512], start=(e_ == 0), stop=(e_ == 15))
            p.ts(ssb[:, o * 512:(o + 1) * 512], pm[:, :], 0.5, ALU.mult)
        p.tt(ssb[:, :], ssb[:, :], ld[:, :], ALU.add)
        p.tt(ssb[:, :], ssb[:, :], ld2[:, :], ALU.add)
        p.red(r1[:, :], ssb[:, :], "sum")
        p.act(r1[:, :], r1[:, :], AF.Copy, scale=1.0 / 4096.0)
        p.ts(ssb[:, :], ssb[:, :], r1[:, :], ALU.subtract)
        p.act(tmp[:, :], ssb[:, :], AF.Square)
        p.red(r2[:, :], tmp[:, :], "sum")
        p.act(r2[:, :], r2[:, :], AF.Ln, bias=1e-5, scale=1.0 / 4096.0)
        p.act(r2[:, :], r2[:, :], AF.Exp, scale=-0.5)
        p.ts(ssb[:, :], ssb[:, :], r2[:, :], ALU.mult)
        p.dma(xo[sl, :], ssb[:, :])
    nc_ = p.finalize()
    for c in reversed(list(ctx.values())):
        c.__exit__(None, None, None)
    return nc_



def kernel(L0_attn_hidden_states, L0_attn_in_proj_w, L0_attn_conv_w, L0_attn_conv_b,
           L0_attn_dt_bias, L0_attn_A_log, L0_attn_D, L0_attn_norm_w, L0_attn_out_proj_w,
           L0_ffn_base_w, L0_ffn_base_b, L0_ffn_A, L0_ffn_B, L0_ffn_router_w):
    cores = list(range(NC))
    f32 = np.float32
    X = _c(np.asarray(L0_attn_hidden_states, dtype=f32).reshape(16384, 4096))
    W = np.asarray(L0_attn_in_proj_w, dtype=f32)
    CW = np.asarray(L0_attn_conv_w, dtype=f32)
    CB = np.asarray(L0_attn_conv_b, dtype=f32)
    DTB = np.asarray(L0_attn_dt_bias, dtype=f32)
    ALOG = np.asarray(L0_attn_A_log, dtype=f32)
    DD = np.asarray(L0_attn_D, dtype=f32)
    NW = np.asarray(L0_attn_norm_w, dtype=f32)
    OW = np.asarray(L0_attn_out_proj_w, dtype=f32)
    BW = np.asarray(L0_ffn_base_w, dtype=f32)
    BB = np.asarray(L0_ffn_base_b, dtype=f32)
    FA = np.asarray(L0_ffn_A, dtype=f32)
    FB = np.asarray(L0_ffn_B, dtype=f32)
    RW = np.asarray(L0_ffn_router_w, dtype=f32)

    xT = _c(X.T)
    WcT, cwf, cbf, acol, dtbc, drow, nwfm, owT = [], [], [], [], [], [], [], []
    for k in range(NC):
        rows = np.concatenate([W[1024 * k:1024 * k + 1024],
                               W[8192 + 1024 * k:8192 + 1024 * k + 1024],
                               W[16384 + 128 * k:16384 + 128 * k + 128],
                               W[17408 + 128 * k:17408 + 128 * k + 128],
                               W[18432 + 16 * k:18432 + 16 * k + 16],
                               np.zeros((240, 4096), f32)], axis=0)
        WcT.append(_c(rows.T))
        cwr = np.concatenate([CW[1024 * k:1024 * k + 1024],
                              CW[8192 + 128 * k:8192 + 128 * k + 128],
                              CW[9216 + 128 * k:9216 + 128 * k + 128]], axis=0)
        cwf.append(_c(cwr.reshape(10, 128, 4).transpose(1, 0, 2).reshape(128, 40)))
        cbr = np.concatenate([CB[1024 * k:1024 * k + 1024],
                              CB[8192 + 128 * k:8192 + 128 * k + 128],
                              CB[9216 + 128 * k:9216 + 128 * k + 128]])
        cbf.append(_c(cbr.reshape(10, 128).T))
        acol.append(_c(ALOG[16 * k:16 * k + 16].reshape(16, 1)))
        dtbc.append(_c(DTB[16 * k:16 * k + 16].reshape(16, 1)))
        drow.append(_c(DD[16 * k:16 * k + 16].reshape(1, 16)))
        nwfm.append(_c(NW[1024 * k:1024 * k + 1024].reshape(8, 128).T))
        owT.append(_c(OW[:, 1024 * k:1024 * k + 1024].T))
    bwT = _c(BW.T)
    rT = _c(RW.T)
    aT = _c(FA.T)
    bT = _c(FB.transpose(0, 2, 1).reshape(1024, 4096))
    ident = np.eye(128, dtype=f32)
    mask = _c(np.triu(np.ones((128, 128), f32)))
    ones1 = np.ones((1, 128), f32)
    e127 = np.zeros((128, 1), f32); e127[127, 0] = 1.0
    ones8 = np.ones((8, 128), f32)
    bbrow = _c(BB.reshape(1, 4096))

    prep = build_prep()
    im = [{"xt_f": _c(xT[:, 2048 * k:2048 * k + 2048]), "w_f": WcT[k], "ow_f": owT[k],
           "bw_f": bwT, "rt_f": rT, "at_f": aT, "bt_f": bT} for k in range(NC)]
    r = run_bass_kernel_spmd(prep, im, core_ids=cores)
    xTb = np.concatenate([r.results[k]["xt_b"] for k in range(NC)], axis=1)
    Wb = [r.results[k]["w_b"] for k in range(NC)]
    owb = [r.results[k]["ow_b"] for k in range(NC)]
    bwb = r.results[0]["bw_b"]
    rtb = r.results[0]["rt_b"]
    atb = r.results[0]["at_b"]
    btb = r.results[0]["bt_b"]

    progA = build_A()
    zxb = [np.empty((2560, 16384), f32) for _ in range(NC)]
    for j in range(16):
        xblk = _c(xTb[:, 1024 * j:1024 * j + 1024])
        for h in range(2):
            im = [{"w_b": _c(Wb[k][:, 1280 * h:1280 * h + 1280]), "xt_b": xblk}
                  for k in range(NC)]
            r = run_bass_kernel_spmd(progA, im, core_ids=cores)
            for k in range(NC):
                zxb[k][1280 * h:1280 * h + 1280, 1024 * j:1024 * j + 1024] = r.results[k]["zxb"]

    progB = build_B()
    Ytm = np.empty((16384, 8192), f32)
    pssq_blk = [np.empty((8, 1024), f32) for _ in range(16)]
    state = [np.zeros((128, 1024), f32) for _ in range(NC)]
    for j in range(16):
        im = []
        for k in range(NC):
            if j == 0:
                zin = np.concatenate([np.zeros((2320, 3), f32),
                                      zxb[k][0:2320, 0:1024]], axis=1)
            else:
                zin = zxb[k][0:2320, 1024 * j - 3:1024 * j + 1024]
            im.append({"zxb": _c(zin), "cw": cwf[k], "cb": cbf[k], "acol": acol[k],
                       "dtb": dtbc[k], "drow": drow[k], "st_in": state[k],
                       "ident": ident, "mask": mask, "ones1": ones1, "e127": e127})
        r = run_bass_kernel_spmd(progB, im, core_ids=cores)
        for k in range(NC):
            Ytm[1024 * j:1024 * j + 1024, 1024 * k:1024 * k + 1024] = r.results[k]["yg"]
            pssq_blk[j][k, :] = r.results[k]["pssq"].T.reshape(1024)
            state[k] = _c(r.results[k]["st_out"])
    Yfm = _c(Ytm.T)

    progD = build_D()
    pout = [[None] * 16 for _ in range(NC)]
    for j in range(16):
        im = [{"yfm": _c(Yfm[1024 * k:1024 * k + 1024, 1024 * j:1024 * j + 1024]),
               "sq8": _c(pssq_blk[j]), "nwfm": nwfm[k], "ow_b": owb[k], "ones8": ones8}
              for k in range(NC)]
        r = run_bass_kernel_spmd(progD, im, core_ids=cores)
        for k in range(NC):
            pout[k][j] = r.results[k]["pout"]

    progE = build_E()
    x1f = np.empty((16384, 4096), f32)
    x1b = np.empty((16384, 4096), BF)
    for rd in range(2):
        im = []
        for k in range(NC):
            j = rd * 8 + k
            d = {f"pp{i}": pout[i][j] for i in range(8)}
            d["xblk"] = _c(X[1024 * j:1024 * j + 1024, :])
            im.append(d)
        r = run_bass_kernel_spmd(progE, im, core_ids=cores)
        for k in range(NC):
            j = rd * 8 + k
            x1f[1024 * j:1024 * j + 1024] = r.results[k]["x1"]
            x1b[1024 * j:1024 * j + 1024] = r.results[k]["x1b"]
    x1tb = _c(x1b.T)

    progF1 = build_F1()
    bo = [None] * 16
    for rd in range(2):
        im = [{"x1t_b": _c(x1tb[:, 1024 * (rd * 8 + k):1024 * (rd * 8 + k) + 1024]),
               "bw_b": bwb, "bb": bbrow, "ones1": ones1} for k in range(NC)]
        r = run_bass_kernel_spmd(progF1, im, core_ids=cores)
        for k in range(NC):
            bo[rd * 8 + k] = r.results[k]["bo"]

    progF2a = build_F2a()
    rhw = [None] * 16
    for rd in range(2):
        im = [{"x1t_b": _c(x1tb[:, 1024 * (rd * 8 + k):1024 * (rd * 8 + k) + 1024]),
               "rt_b": rtb, "at_b": atb, "ident": ident} for k in range(NC)]
        r = run_bass_kernel_spmd(progF2a, im, core_ids=cores)
        for k in range(NC):
            rhw[rd * 8 + k] = r.results[k]["rhw"]

    progF2b = build_F2b()
    out = np.empty((16384, 4096), f32)
    for rd in range(2):
        im = []
        for k in range(NC):
            j = rd * 8 + k
            im.append({"rhw": rhw[j], "bt_b": btb, "bo": bo[j],
                       "x1": _c(x1f[1024 * j:1024 * j + 1024])})
        r = run_bass_kernel_spmd(progF2b, im, core_ids=cores)
        for k in range(NC):
            j = rd * 8 + k
            out[1024 * j:1024 * j + 1024] = r.results[k]["xo"]
    return out.reshape(1, 16384, 4096)
